# version 35
# baseline (speedup 1.0000x reference)
import numpy as np
import concourse.bass as bass
import concourse.mybir as mybir
from concourse.bass_utils import run_bass_kernel_spmd
from contextlib import ExitStack

F32 = mybir.dt.float32
BF16 = mybir.dt.bfloat16
I32 = mybir.dt.int32
AF = mybir.ActivationFunctionType
ALU = mybir.AluOpType

ENGINES = ("tensor", "vector", "scalar", "gpsimd", "sync")

P = 128
TC = 256
NCH = 8
FD = NCH * TC
PITCH = TC + 1
NSEQ = 4
NW = 2048 // TC
NCHUNK = NSEQ * NW
HEPS = 1e-5
QUAKE = 0x5F3759DF



class _Region:
    __slots__ = ("lo", "hi", "writer", "readers")

    def __init__(self, lo, hi, writer, readers):
        self.lo, self.hi, self.writer, self.readers = lo, hi, writer, readers


class Prog:
    def __init__(self, nc, serialize=False):
        self.nc = nc
        self.serialize = serialize
        self.sem = {}
        self.counts = {e: 0 for e in ENGINES}
        self.chan = {}
        self.clock = {e: {} for e in ENGINES}
        self.snap = {}
        self.regions = {}
        self.psum_full = {}
        self.n_waits = 0
        self.n_ops = 0
        self.last_event = None
        self._es = None
        self.ev_desc = {}
        self.wait_log = []
        self.cur_desc = ""
        self.graph = []
        self.cur_kind = ""

    def setup(self, es):
        self._es = es
        for e in ENGINES:
            self.sem[e] = es.enter_context(self.nc.semaphore(f"s_{e}"))

    def channel(self, name, engine):
        if name not in self.chan:
            sem = self._es.enter_context(self.nc.semaphore(f"c_{name}"))
            self.chan[name] = dict(engine=engine, sem=sem, count=0)
        return name

    def ap_range(self, ap):
        t = ap.tensor
        name = t.name
        if name in self.psum_full:
            return name, 0, self.psum_full[name]
        esz = mybir.dt.size(t.dtype)
        dims = list(ap.ap)
        off = int(ap.offset)
        span = 0
        for step, count in dims[1:]:
            span += abs(int(step)) * (int(count) - 1)
        return name, off * esz, (off + span + 1) * esz

    def _dep_events(self, name, lo, hi, is_write):
        evs = []
        for r in self.regions.get(name, ()):
            if r.hi <= lo or r.lo >= hi:
                continue
            if r.writer is not None:
                evs.append(r.writer)
            if is_write:
                evs.extend(r.readers)
        return evs

    def _record_read(self, name, lo, hi, ev):
        for r in self.regions.setdefault(name, []):
            if not (r.hi <= lo or r.lo >= hi):
                r.readers.append(ev)

    def _record_write(self, name, lo, hi, ev):
        lst = self.regions.setdefault(name, [])
        out = []
        for r in lst:
            if r.hi <= lo or r.lo >= hi:
                out.append(r)
                continue
            if r.lo < lo:
                out.append(_Region(r.lo, lo, r.writer, list(r.readers)))
            if r.hi > hi:
                out.append(_Region(hi, r.hi, r.writer, list(r.readers)))
        out.append(_Region(lo, hi, ev, []))
        self.regions[name] = out

    def _sem_by_name(self, semname):
        kind, nm = semname.split(":", 1)
        return self.sem[nm] if kind == "s" else self.chan[nm]["sem"]

    def _wait(self, engine, ev):
        semname, val = ev
        cl = self.clock[engine]
        if cl.get(semname, 0) >= val:
            return
        getattr(self.nc, engine).wait_ge(self._sem_by_name(semname), val)
        self.n_waits += 1
        self.wait_log.append((engine, semname, val, self.cur_desc,
                              self.ev_desc.get((semname, val), "?")))
        for k, v in self.snap.get(ev, {}).items():
            if cl.get(k, 0) < v:
                cl[k] = v
        cl[semname] = max(cl.get(semname, 0), val)

    def _emit_deps(self, engine, deps):
        byname = {}
        for semname, val in deps:
            if semname == "s:" + engine and engine == "tensor":
                continue
            if byname.get(semname, 0) < val:
                byname[semname] = val
        for semname, val in sorted(byname.items()):
            self._wait(engine, (semname, val))

    def _collect(self, reads, writes):
        deps = []
        rw = []
        for ap in reads:
            name, lo, hi = self.ap_range(ap)
            deps.extend(self._dep_events(name, lo, hi, False))
            rw.append((name, lo, hi, False))
        for ap in writes:
            name, lo, hi = self.ap_range(ap)
            deps.extend(self._dep_events(name, lo, hi, True))
            rw.append((name, lo, hi, True))
        if self.serialize and self.last_event is not None:
            deps.append(self.last_event)
        return deps, rw

    def _finish_op(self, engine, ev, rw, clock_extra=None):
        cl = dict(self.clock[engine])
        if clock_extra:
            cl.update(clock_extra)
        self.snap[ev] = cl
        self.last_event = ev
        self.n_ops += 1
        for name, lo, hi, w in rw:
            (self._record_write if w else self._record_read)(name, lo, hi, ev)

    def op(self, engine, fn, reads=(), writes=()):
        deps, rw = self._collect(reads, writes)
        self.graph.append((None, engine, self.cur_desc, list(deps), self.cur_kind))
        self._emit_deps(engine, deps)
        inst = fn()
        try:
            iname = type(inst.ins).__name__
        except Exception:
            iname = "?"
        g = self.graph[-1]
        self.graph[-1] = (g[0], g[1], g[2], g[3], iname)
        self.counts[engine] += 1
        n = self.counts[engine]
        inst.then_inc(self.sem[engine], 1)
        ev = ("s:" + engine, n)
        self.ev_desc[ev] = self.cur_desc
        self.graph[-1] = (ev,) + self.graph[-1][1:]
        self._finish_op(engine, ev, rw, clock_extra={"s:" + engine: n})
        return ev

    def dma(self, chname, fn, reads=(), writes=()):
        ch = self.chan[chname]
        engine = ch["engine"]
        deps, rw = self._collect(reads, writes)
        if ch["count"] > 0:
            deps.append(("c:" + chname, 16 * ch["count"]))
        self.graph.append((None, "dma:" + chname, self.cur_desc, list(deps),
                           self.cur_kind))
        self._emit_deps(engine, deps)
        inst = fn()
        ch["count"] += 1
        inst.then_inc(ch["sem"], 16)
        ev = ("c:" + chname, 16 * ch["count"])
        self.graph[-1] = (ev,) + self.graph[-1][1:]
        self._finish_op(engine, ev, rw)
        return ev

    def matmul_group(self, psum_ap, pairs):
        n = len(pairs)
        ev = None
        for i, (lh, rh) in enumerate(pairs):
            ev = self.op(
                "tensor",
                lambda lh=lh, rh=rh, i=i: self.nc.tensor.matmul(
                    psum_ap, lh, rh, start=(i == 0), stop=(i == n - 1)
                ),
                reads=[lh, rh],
                writes=[psum_ap] if i in (0, n - 1) else (),
            )
        return ev

    def finish(self, engine="sync"):
        for o in ENGINES:
            if self.counts[o] > 0 and o != engine:
                self._wait(engine, ("s:" + o, self.counts[o]))
        for nm, ch in self.chan.items():
            if ch["count"] > 0:
                self._wait(engine, ("c:" + nm, 16 * ch["count"]))


def mkap(handle, off, dims):
    base = handle[:]
    return bass.AP(base.tensor, off, dims)


W_ORDER = ["wv", "wi", "wf", "wo", "wr", "wk", "wv2"]


def build_program(serialize=False):
    nc = bass.Bass("TRN2", target_bir_lowering=False,
                   dynamic_dma_scratch_size=512)
    xt = nc.dram_tensor("xt", [1024, NSEQ * 2048], F32, kind="ExternalInput").ap()
    wdr = {w: nc.dram_tensor(w, [1024, 1024], F32, kind="ExternalInput").ap()
           for w in W_ORDER}
    mur_d = nc.dram_tensor("mur", [P, NCH], F32, kind="ExternalInput").ap()
    muk_d = nc.dram_tensor("muk", [P, NCH], F32, kind="ExternalInput").ap()
    yt = nc.dram_tensor("yt", [1024, NSEQ * 2048], F32, kind="ExternalOutput").ap()

    p = Prog(nc, serialize=serialize)
    es = ExitStack()
    with es:
        p.setup(es)
        sb_ = lambda n, s, d=F32: es.enter_context(nc.sbuf_tensor(n, s, d))

        ws = {w: sb_("ws_" + w, [P, NCH * 1024], BF16) for w in W_ORDER}
        S0s = [sb_(f"s0_{i}", [P, NCH * PITCH]) for i in range(3)]
        S1s = [sb_(f"s1_{i}", [P, NCH * PITCH]) for i in range(2)]
        S2s = [sb_(f"s2_{i}", [P, NCH * PITCH]) for i in range(2)]
        S3s = [sb_(f"s3_{i}", [P, NCH * PITCH]) for i in range(2)]
        XBs = [sb_(f"xb_{i}", [P, FD], BF16) for i in range(2)]
        B1s = [sb_(f"b1_{i}", [P, FD], BF16) for i in range(2)]
        B2s = [sb_(f"b2_{i}", [P, FD], BF16) for i in range(2)]
        Xbs = [sb_(f"xrow_{i}", [P, TC]) for i in range(2)]
        Tbs = [sb_(f"trow_{i}", [P, TC]) for i in range(2)]
        Ybs = [sb_(f"yrow_{i}", [P, TC]) for i in range(2)]
        Qibs = [sb_(f"qib_{i}", [P, TC], I32) for i in range(2)]
        qC = sb_("qc", [P, TC], I32)
        ones128 = sb_("ones128", [P, P], BF16)
        eps_row = sb_("eps_row", [1, TC])
        mur = sb_("mur_s", [P, NCH])
        muk = sb_("muk_s", [P, NCH])
        stc = sb_("stc", [P, NSEQ * NCH])
        ybc = sb_("ybc", [P, NSEQ * NCH])
        onesb = sb_("onesb", [P, 1], BF16)
        onesf = sb_("onesf", [1, P])

        def ps_(n, s):
            t = es.enter_context(nc.psum_tensor(n, s, F32))
            p.psum_full[n] = 2048 * 4
            return t

        pmrot = [ps_(f"pm{i}", [P, 512]) for i in range(6)]
        pmA = ps_("pmA", [P, 512])
        pmB = ps_("pmB", [P, 512])

        rot_i = [0]

        def next_rot():
            t = pmrot[rot_i[0] % 6]
            rot_i[0] += 1
            return t[:, 0:TC]

        v, a, g, sy = nc.vector, nc.scalar, nc.gpsimd, nc.sync

        def flat(slot, n=None):
            if n is None:
                return slot[:, 0:FD]
            return slot[:, n * TC:(n + 1) * TC]

        def flat3(slot):
            return flat(slot).rearrange("p (a t) -> p a t", a=NCH)

        def flat_k(ap2d, k=NCH):
            return ap2d.rearrange("p (k t) -> p k t", k=k)

        p.op("vector", lambda: v.memset(onesb[:], 1.0), [], [onesb[:]])
        p.op("vector", lambda: v.memset(onesf[:], 1.0), [], [onesf[:]])
        p.op("vector", lambda: v.memset(ones128[:], 2.0 ** -10), [], [ones128[:]])
        p.op("vector", lambda: v.memset(qC[:], QUAKE), [], [qC[:]])
        p.op("vector", lambda: v.memset(eps_row[:], HEPS), [], [eps_row[:]])
        cmu = p.channel("cmu", "sync")
        p.dma(cmu, lambda: sy.dma_start(out=mur[:], in_=mur_d), [mur_d], [mur[:]])
        p.dma(cmu, lambda: sy.dma_start(out=muk[:], in_=muk_d), [muk_d], [muk[:]])

        xin = [p.channel(f"xin{i}", "sync") for i in range(3)]
        stq = [p.channel(f"st{i}", "sync") for i in range(3)]
        wl = [p.channel("wl0", "sync"), p.channel("wl1", "sync")]

        def x_src(base):
            return xt[:, base:base + TC].rearrange("(k p) t -> p k t", p=P)

        def y_dst(base):
            return yt[:, base:base + TC].rearrange("(k p) t -> p k t", p=P)

        p.dma(xin[0], lambda: sy.dma_start(out=flat_k(flat(S0s[0])),
                                           in_=x_src(0)),
              [xt], [flat(S0s[0])])
        p.op("vector", lambda: v.tensor_copy(XBs[0][:], flat(S0s[0])),
             [flat(S0s[0])], [XBs[0][:]])
        p.dma(xin[1], lambda: sy.dma_start(out=flat_k(flat(S0s[1])),
                                           in_=x_src(2048)),
              [xt], [flat(S0s[1])])

        for i, w in enumerate(W_ORDER):
            for q in range(4):
                j = 4 * i + q
                stage = [S2s[0], S3s[0], S2s[1], S3s[1]][j % 4]
                src = wdr[w][256 * q:256 * (q + 1), :].rearrange(
                    "(k p) n -> p k n", p=P)
                ch = wl[j % 2]
                qeng = sy
                p.dma(ch, lambda qeng=qeng, stage=stage, src=src:
                      qeng.dma_start(out=flat_k(flat(stage), k=2), in_=src),
                      [wdr[w]], [flat(stage)])
                dst = ws[w][:, q * 2048:(q + 1) * 2048]
                if w == "wv2":
                    p.op("vector", lambda dst=dst, stage=stage:
                         v.tensor_scalar_mul(dst, flat(stage), 0.5),
                         [flat(stage)], [dst])
                else:
                    p.op("vector", lambda dst=dst, stage=stage:
                         v.tensor_copy(dst, flat(stage)), [flat(stage)], [dst])

        def mm_into(wname, m, rhs_buf, psum_ap):
            pairs = []
            for k in range(NCH):
                lh = ws[wname][:, k * 1024 + m * P: k * 1024 + (m + 1) * P]
                rh = rhs_buf[:, k * TC:(k + 1) * TC]
                pairs.append((lh, rh))
            return p.matmul_group(psum_ap, pairs)

        def bufs(c):
            pa = c % 2
            return dict(
                S0=S0s[c % 3], S1=S1s[pa], S2=S2s[pa], S3=S3s[pa],
                XB=XBs[pa], B1=B1s[pa], B2=B2s[pa],
                Xb=Xbs[pa], Tb=Tbs[pa], Yb=Ybs[pa], Qib=Qibs[pa],
            )

        def chunk_pos(c):
            w_i, b = divmod(c, NSEQ)
            return b, w_i

        def y1aps(S0):
            cur = mkap(S0, 1, [[NCH * PITCH, P], [PITCH, NCH], [1, TC]])
            prev = mkap(S0, 0, [[NCH * PITCH, P], [PITCH, NCH], [1, TC]])
            bnd = mkap(S0, 0, [[NCH * PITCH, P], [PITCH, NCH], [1, 1]])
            last = mkap(S0, TC, [[NCH * PITCH, P], [PITCH, NCH], [1, 1]])
            full = S0[:, 0:NCH * PITCH]
            return cur, prev, bnd, last, full

        def stage_A1(c, __name='stage_A1'):
            p.cur_desc = f'stage_A1({c})'
            b, w_i = chunk_pos(c)
            base = b * 2048 + w_i * TC
            S0 = bufs(c)["S0"]
            p.dma(xin[c % 3], lambda S0=S0, base=base: sy.dma_start(
                out=flat_k(flat(S0)), in_=x_src(base)), [xt], [flat(S0)])

        def stage_A2(c, __name='stage_A2'):
            p.cur_desc = f'stage_A2({c})'
            B = bufs(c)
            S0, XB = B["S0"], B["XB"]
            hq = NCH // 2
            p.op("vector", lambda S0=S0, XB=XB, hq=hq: v.tensor_copy(
                XB[:, 0:hq * TC], flat(S0)[:, 0:hq * TC]),
                 [flat(S0)[:, 0:hq * TC]], [XB[:, 0:hq * TC]])
            p.op("vector", lambda S0=S0, XB=XB, hq=hq: v.tensor_copy(
                XB[:, hq * TC:FD], flat(S0)[:, hq * TC:FD]),
                 [flat(S0)[:, hq * TC:FD]], [XB[:, hq * TC:FD]])

        def stage_VIF(c, __name='stage_VIF'):
            p.cur_desc = f'stage_VIF({c})'
            B = bufs(c)
            S0, S1, S2, S3, XB = B["S0"], B["S1"], B["S2"], B["S3"], B["XB"]
            TW = S3s[(c + 1) % 2]
            b, w_i = chunk_pos(c)
            for n in range(NCH):
                pmV = next_rot()
                mm_into("wv", n, XB[:], pmV)
                p.op("scalar", lambda n=n, pmV=pmV: a.activation(
                    flat(S1, n), pmV, AF.Silu), [pmV], [flat(S1, n)])
                pmI = next_rot()
                mm_into("wi", n, XB[:], pmI)
                p.op("scalar", lambda n=n, pmI=pmI: a.activation(
                    flat(TW, n), pmI, AF.Tanh, scale=0.5),
                    [pmI], [flat(TW, n)])
                p.op("vector", lambda n=n: v.scalar_tensor_tensor(
                    flat(S2, n), flat(TW, n), 1.0, flat(S1, n),
                    ALU.add, ALU.mult),
                    [flat(TW, n), flat(S1, n)], [flat(S2, n)])
                pmF = next_rot()
                mm_into("wf", n, XB[:], pmF)
                p.op("scalar", lambda n=n, pmF=pmF: a.activation(
                    flat(S3, n), pmF, AF.Tanh), [pmF], [flat(S3, n)])
            for n in range(NCH):
                init = 0.0 if w_i == 0 else stc[:, b * NCH + n: b * NCH + n + 1]
                rd = [flat(S3, n), flat(S2, n)]
                if w_i != 0:
                    rd.append(stc[:, b * NCH + n: b * NCH + n + 1])
                p.op("vector", lambda n=n, init=init: v.tensor_tensor_scan(
                    flat(S3, n), flat(S3, n), flat(S2, n), init,
                    ALU.mult, ALU.add), rd, [flat(S3, n)])
                p.op("scalar", lambda n=n: a.activation(
                    B["B1"][:, n * TC:(n + 1) * TC], flat(S3, n), AF.Silu,
                    scale=0.5),
                     [flat(S3, n)], [B["B1"][:, n * TC:(n + 1) * TC]])
            if w_i != NW - 1:
                sco = mkap(S3, TC - 1, [[NCH * PITCH, P], [TC, NCH], [1, 1]])
                std = mkap(stc, b * NCH, [[NSEQ * NCH, P], [1, NCH], [1, 1]])
                p.op("gpsimd", lambda sco=sco, std=std: g.tensor_copy(std, sco),
                     [flat(S3)], [stc[:, b * NCH:(b + 1) * NCH]])

        def stage_O(c, __name='stage_O'):
            p.cur_desc = f'stage_O({c})'
            B = bufs(c)
            S0, S2 = B["S0"], B["S2"]
            for h in range(NCH):
                pm = next_rot()
                mm_into("wo", h, B["B1"][:], pm)
                p.op("vector", lambda h=h, pm=pm: v.tensor_tensor(
                    flat(S2, h), pm, flat(S0, h), ALU.add),
                    [pm, flat(S0, h)], [flat(S2, h)])

        def ln_stats(c, src_slot, hbf, sqb, B, bank):
            Xr, Tr, Yr, Qi = B["Xb"][:], B["Tb"][:], B["Yb"][:], B["Qib"][:]
            m_bc = bank[:, 0:TC]
            q_bc = bank[:, TC:2 * TC]
            hh = NCH // 2
            p.op("scalar", lambda hh=hh: a.copy(
                hbf[:, 0:hh * TC], flat(src_slot)[:, 0:hh * TC]),
                 [flat(src_slot)[:, 0:hh * TC]], [hbf[:, 0:hh * TC]])
            p.op("scalar", lambda hh=hh: a.copy(
                hbf[:, hh * TC:FD], flat(src_slot)[:, hh * TC:FD]),
                 [flat(src_slot)[:, hh * TC:FD]], [hbf[:, hh * TC:FD]])
            p.op("scalar", lambda hh=hh: a.activation(
                sqb[:, 0:hh * TC], flat(src_slot)[:, 0:hh * TC], AF.Square),
                 [flat(src_slot)[:, 0:hh * TC]], [sqb[:, 0:hh * TC]])
            p.op("scalar", lambda hh=hh: a.activation(
                sqb[:, hh * TC:FD], flat(src_slot)[:, hh * TC:FD], AF.Square),
                 [flat(src_slot)[:, hh * TC:FD]], [sqb[:, hh * TC:FD]])
            p.matmul_group(m_bc, [(ones128[:], hbf[:, k * TC:(k + 1) * TC])
                                  for k in range(NCH)])
            p.matmul_group(q_bc, [(ones128[:], sqb[:, k * TC:(k + 1) * TC])
                                  for k in range(NCH)]
                           + [(onesf[:], eps_row[:])])
            p.op("scalar", lambda: a.activation(Tr, m_bc, AF.Square),
                 [m_bc], [Tr])
            p.op("vector", lambda: v.scalar_tensor_tensor(
                Xr, Tr, -1.0, q_bc, ALU.mult, ALU.add), [Tr, q_bc], [Xr])
            p.op("vector", lambda: v.tensor_single_scalar(
                Qi, Xr.bitcast(I32), 1, ALU.arith_shift_right), [Xr], [Qi])
            p.op("vector", lambda: v.tensor_tensor(Qi, qC[:], Qi, ALU.subtract),
                 [qC[:], Qi], [Qi])
            ysrc = Qi.bitcast(F32)
            for it in range(3):
                yin = ysrc if it == 0 else Yr
                yrd = [Qi] if it == 0 else [Yr]
                p.op("vector", lambda yin=yin, yrd=yrd: v.tensor_tensor(
                    Tr, yin, yin, ALU.mult), yrd, [Tr])
                p.op("vector", lambda: v.scalar_tensor_tensor(
                    Tr, Tr, -0.5, Xr, ALU.mult, ALU.mult), [Tr, Xr], [Tr])
                p.op("vector", lambda yin=yin, yrd=yrd: v.scalar_tensor_tensor(
                    Yr, Tr, 1.5, yin, ALU.add, ALU.mult),
                    [Tr] + yrd, [Yr])

        def ln_apply(c, src_slot, B, out3, out_writes, bank):
            Yr = B["Yb"][:]
            src3 = flat3(src_slot)
            bm3 = bank[:, 0:TC].rearrange("p (a t) -> p a t", a=1)
            yr3 = Yr.rearrange("p (a t) -> p a t", a=1)
            s3b, bm3b = bass.broadcast_tensor_aps(src3, bm3)
            p.op("vector", lambda: v.tensor_tensor(s3b, s3b, bm3b,
                                                   ALU.subtract),
                 [flat(src_slot), bank[:, 0:TC]], [flat(src_slot)])
            t3b, yr3b = bass.broadcast_tensor_aps(src3, yr3)
            p.op("vector", lambda: v.tensor_tensor(out3, t3b, yr3b, ALU.mult),
                 [flat(src_slot), Yr], out_writes)

        def stage_E(c, __name='stage_E'):
            p.cur_desc = f'stage_E({c})'
            B = bufs(c)
            ln_stats(c, B["S2"], B["B2"], B["B1"], B, pmA)

        def stage_F(c, __name='stage_F'):
            p.cur_desc = f'stage_F({c})'
            B = bufs(c)
            S0, S3, XB, B2 = B["S0"], B["S3"], B["XB"], B["B2"]
            b, w_i = chunk_pos(c)
            y1cur, y1prev, y1bnd, y1last, y1full = y1aps(S0)
            ln_apply(c, B["S2"], B, y1cur, [y1full], pmA)
            if w_i == 0:
                p.op("gpsimd", lambda: g.memset(y1bnd, 0.0), [], [y1full])
            else:
                ybs = mkap(ybc, b * NCH, [[NSEQ * NCH, P], [1, NCH], [1, 1]])
                p.op("gpsimd", lambda ybs=ybs: g.tensor_copy(y1bnd, ybs),
                     [ybc[:, b * NCH:(b + 1) * NCH]], [y1full])
            S1 = B["S1"]
            half = NCH // 2
            y1c_a = mkap(S0, 1, [[NCH * PITCH, P], [PITCH, half], [1, TC]])
            y1p_a = mkap(S0, 0, [[NCH * PITCH, P], [PITCH, half], [1, TC]])
            y1c_b = mkap(S0, half * PITCH + 1,
                         [[NCH * PITCH, P], [PITCH, half], [1, TC]])
            y1p_b = mkap(S0, half * PITCH,
                         [[NCH * PITCH, P], [PITCH, half], [1, TC]])
            d3a = flat(S1)[:, 0:half * TC].rearrange("p (a t) -> p a t", a=half)
            d3b = flat(S1)[:, half * TC:FD].rearrange("p (a t) -> p a t", a=half)
            p.op("vector", lambda: v.tensor_tensor(
                d3a, y1c_a, y1p_a, ALU.subtract),
                [y1full], [flat(S1)[:, 0:half * TC]])
            p.op("vector", lambda: v.tensor_tensor(
                d3b, y1c_b, y1p_b, ALU.subtract),
                [y1full], [flat(S1)[:, half * TC:FD]])
            for n in range(NCH):
                yp = mkap(S0, n * PITCH, [[NCH * PITCH, P], [1, TC]])
                p.op("vector", lambda n=n, yp=yp: v.scalar_tensor_tensor(
                    B2[:, n * TC:(n + 1) * TC], flat(S1, n),
                    mur[:, n:n + 1], yp, ALU.mult, ALU.add),
                    [flat(S1, n), mur[:, n:n + 1], y1full],
                    [B2[:, n * TC:(n + 1) * TC]])
            for n in range(NCH):
                yp = mkap(S0, n * PITCH, [[NCH * PITCH, P], [1, TC]])
                p.op("vector", lambda n=n, yp=yp: v.scalar_tensor_tensor(
                    XB[:, n * TC:(n + 1) * TC], flat(S1, n),
                    muk[:, n:n + 1], yp, ALU.mult, ALU.add),
                    [flat(S1, n), muk[:, n:n + 1], y1full],
                    [XB[:, n * TC:(n + 1) * TC]])
            if w_i != NW - 1:
                ybd = mkap(ybc, b * NCH, [[NSEQ * NCH, P], [1, NCH], [1, 1]])
                p.op("gpsimd", lambda ybd=ybd: g.tensor_copy(ybd, y1last),
                     [y1full], [ybc[:, b * NCH:(b + 1) * NCH]])

        def stage_GH_R(c, __name='stage_GH_R'):
            p.cur_desc = f'stage_GH_R({c})'
            B = bufs(c)
            S0, S1, S2, S3, XB = B["S0"], B["S1"], B["S2"], B["S3"], B["XB"]
            for n in range(NCH):
                pm = next_rot()
                mm_into("wr", n, B["B2"][:], pm)
                p.op("scalar", lambda n=n, pm=pm: a.activation(
                    flat(S1, n), pm, AF.Tanh, scale=0.5), [pm], [flat(S1, n)])
            for n in range(NCH):
                pm = next_rot()
                mm_into("wk", n, XB[:], pm)
                p.op("scalar", lambda n=n, pm=pm: a.activation(
                    flat(S2, n), pm, AF.Relu), [pm], [flat(S2, n)])
                p.op("scalar", lambda n=n: a.activation(
                    B["B1"][:, n * TC:(n + 1) * TC], flat(S2, n), AF.Square),
                    [flat(S2, n)], [B["B1"][:, n * TC:(n + 1) * TC]])
        def stage_GH_V(c, __name='stage_GH_V'):
            p.cur_desc = f'stage_GH_V({c})'
            B = bufs(c)
            S0, S1, S2, S3 = B["S0"], B["S1"], B["S2"], B["S3"]
            for d in range(NCH):
                pm = next_rot()
                mm_into("wv2", d, B["B1"][:], pm)
                p.op("vector", lambda d=d, pm=pm: v.scalar_tensor_tensor(
                    flat(S3, d), flat(S1, d), 1.0, pm, ALU.add, ALU.mult),
                    [pm, flat(S1, d)], [flat(S3, d)])
                yc = mkap(S0, d * PITCH + 1, [[NCH * PITCH, P], [1, TC]])
                p.op("gpsimd", lambda d=d, yc=yc: g.tensor_tensor(
                    flat(S2, d), flat(S3, d), yc, ALU.add),
                    [flat(S3, d), S0[:, 0:NCH * PITCH]], [flat(S2, d)])

        def stage_I(c, __name='stage_I'):
            p.cur_desc = f'stage_I({c})'
            B = bufs(c)
            ln_stats(c, B["S2"], B["B2"], B["B1"], B, pmB)

        def stage_JK(c, __name='stage_JK'):
            p.cur_desc = f'stage_JK({c})'
            B = bufs(c)
            S0 = B["S0"]
            b, w_i = chunk_pos(c)
            base = b * 2048 + w_i * TC
            ln_apply(c, B["S2"], B, flat3(S0), [flat(S0)], pmB)
            p.dma(stq[c % 3],
                  lambda base=base, S0=S0: sy.dma_start(out=y_dst(base),
                                                        in_=flat_k(flat(S0))),
                  [flat(S0)], [yt])

        for c in range(NCHUNK + 1):
            if c < NCHUNK:
                stage_VIF(c)
            if 1 <= c <= NCHUNK:
                stage_GH_R(c - 1)
            if c < NCHUNK:
                stage_O(c)
            if 1 <= c <= NCHUNK:
                stage_GH_V(c - 1)
            if c + 1 < NCHUNK:
                stage_A2(c + 1)
            if c < NCHUNK:
                stage_E(c)
            if c < NCHUNK:
                stage_F(c)
            if 1 <= c <= NCHUNK:
                stage_I(c - 1)
            if 1 <= c <= NCHUNK:
                stage_JK(c - 1)
            if c + 2 < NCHUNK:
                stage_A1(c + 2)

        p.finish("sync")
    return nc, p


def kernel(L0_attn_x, L0_attn_Wf, L0_attn_Wi, L0_attn_Wv, L0_attn_Wo,
           L0_ffn_W_r, L0_ffn_W_k, L0_ffn_W_v, L0_ffn_mu_r, L0_ffn_mu_k):
    x = np.asarray(L0_attn_x, dtype=np.float32)
    wT = {
        "wf": np.ascontiguousarray(np.asarray(L0_attn_Wf, np.float32).T),
        "wi": np.ascontiguousarray(np.asarray(L0_attn_Wi, np.float32).T),
        "wv": np.ascontiguousarray(np.asarray(L0_attn_Wv, np.float32).T),
        "wo": np.ascontiguousarray(np.asarray(L0_attn_Wo, np.float32).T),
        "wr": np.ascontiguousarray(np.asarray(L0_ffn_W_r, np.float32).T),
        "wk": np.ascontiguousarray(np.asarray(L0_ffn_W_k, np.float32).T),
        "wv2": np.ascontiguousarray(np.asarray(L0_ffn_W_v, np.float32).T),
    }
    mur = np.ascontiguousarray(
        np.asarray(L0_ffn_mu_r, np.float32).reshape(NCH, P).T)
    muk = np.ascontiguousarray(
        np.asarray(L0_ffn_mu_k, np.float32).reshape(NCH, P).T)

    nc, _ = build_program()
    in_maps = []
    for c in range(8):
        xtc = np.ascontiguousarray(
            x[4 * c:4 * c + 4].transpose(2, 0, 1).reshape(1024, NSEQ * 2048))
        m = dict(xt=xtc, mur=mur, muk=muk)
        m.update(wT)
        in_maps.append(m)
    res = run_bass_kernel_spmd(nc, in_maps, core_ids=list(range(8)))
    outs = []
    for c in range(8):
        ytc = np.asarray(res.results[c]["yt"])
        outs.append(ytc.reshape(1024, NSEQ, 2048).transpose(1, 2, 0))
    return np.ascontiguousarray(np.concatenate(outs, axis=0))


# version 36
# speedup vs baseline: 1.0071x; 1.0071x over previous
import numpy as np
import concourse.bass as bass
import concourse.mybir as mybir
from concourse.bass_utils import run_bass_kernel_spmd
from contextlib import ExitStack

F32 = mybir.dt.float32
BF16 = mybir.dt.bfloat16
I32 = mybir.dt.int32
AF = mybir.ActivationFunctionType
ALU = mybir.AluOpType

ENGINES = ("tensor", "vector", "scalar", "gpsimd", "sync")

P = 128
TC = 256
NCH = 8
FD = NCH * TC
PITCH = TC + 1
NSEQ = 4
NW = 2048 // TC
NCHUNK = NSEQ * NW
HEPS = 1e-5
QUAKE = 0x5F3759DF



class _Region:
    __slots__ = ("lo", "hi", "writer", "readers")

    def __init__(self, lo, hi, writer, readers):
        self.lo, self.hi, self.writer, self.readers = lo, hi, writer, readers


class Prog:
    def __init__(self, nc, serialize=False):
        self.nc = nc
        self.serialize = serialize
        self.sem = {}
        self.counts = {e: 0 for e in ENGINES}
        self.chan = {}
        self.clock = {e: {} for e in ENGINES}
        self.snap = {}
        self.regions = {}
        self.psum_full = {}
        self.n_waits = 0
        self.n_ops = 0
        self.last_event = None
        self._es = None
        self.ev_desc = {}
        self.wait_log = []
        self.cur_desc = ""
        self.graph = []
        self.cur_kind = ""

    def setup(self, es):
        self._es = es
        for e in ENGINES:
            self.sem[e] = es.enter_context(self.nc.semaphore(f"s_{e}"))

    def channel(self, name, engine):
        if name not in self.chan:
            sem = self._es.enter_context(self.nc.semaphore(f"c_{name}"))
            self.chan[name] = dict(engine=engine, sem=sem, count=0)
        return name

    def ap_range(self, ap):
        t = ap.tensor
        name = t.name
        if name in self.psum_full:
            return name, 0, self.psum_full[name]
        esz = mybir.dt.size(t.dtype)
        dims = list(ap.ap)
        off = int(ap.offset)
        span = 0
        for step, count in dims[1:]:
            span += abs(int(step)) * (int(count) - 1)
        return name, off * esz, (off + span + 1) * esz

    def _dep_events(self, name, lo, hi, is_write):
        evs = []
        for r in self.regions.get(name, ()):
            if r.hi <= lo or r.lo >= hi:
                continue
            if r.writer is not None:
                evs.append(r.writer)
            if is_write:
                evs.extend(r.readers)
        return evs

    def _record_read(self, name, lo, hi, ev):
        for r in self.regions.setdefault(name, []):
            if not (r.hi <= lo or r.lo >= hi):
                r.readers.append(ev)

    def _record_write(self, name, lo, hi, ev):
        lst = self.regions.setdefault(name, [])
        out = []
        for r in lst:
            if r.hi <= lo or r.lo >= hi:
                out.append(r)
                continue
            if r.lo < lo:
                out.append(_Region(r.lo, lo, r.writer, list(r.readers)))
            if r.hi > hi:
                out.append(_Region(hi, r.hi, r.writer, list(r.readers)))
        out.append(_Region(lo, hi, ev, []))
        self.regions[name] = out

    def _sem_by_name(self, semname):
        kind, nm = semname.split(":", 1)
        return self.sem[nm] if kind == "s" else self.chan[nm]["sem"]

    def _wait(self, engine, ev):
        semname, val = ev
        cl = self.clock[engine]
        if cl.get(semname, 0) >= val:
            return
        getattr(self.nc, engine).wait_ge(self._sem_by_name(semname), val)
        self.n_waits += 1
        self.wait_log.append((engine, semname, val, self.cur_desc,
                              self.ev_desc.get((semname, val), "?")))
        for k, v in self.snap.get(ev, {}).items():
            if cl.get(k, 0) < v:
                cl[k] = v
        cl[semname] = max(cl.get(semname, 0), val)

    def _emit_deps(self, engine, deps):
        byname = {}
        for semname, val in deps:
            if semname == "s:" + engine and engine == "tensor":
                continue
            if byname.get(semname, 0) < val:
                byname[semname] = val
        for semname, val in sorted(byname.items()):
            self._wait(engine, (semname, val))

    def _collect(self, reads, writes):
        deps = []
        rw = []
        for ap in reads:
            name, lo, hi = self.ap_range(ap)
            deps.extend(self._dep_events(name, lo, hi, False))
            rw.append((name, lo, hi, False))
        for ap in writes:
            name, lo, hi = self.ap_range(ap)
            deps.extend(self._dep_events(name, lo, hi, True))
            rw.append((name, lo, hi, True))
        if self.serialize and self.last_event is not None:
            deps.append(self.last_event)
        return deps, rw

    def _finish_op(self, engine, ev, rw, clock_extra=None):
        cl = dict(self.clock[engine])
        if clock_extra:
            cl.update(clock_extra)
        self.snap[ev] = cl
        self.last_event = ev
        self.n_ops += 1
        for name, lo, hi, w in rw:
            (self._record_write if w else self._record_read)(name, lo, hi, ev)

    def op(self, engine, fn, reads=(), writes=()):
        deps, rw = self._collect(reads, writes)
        self.graph.append((None, engine, self.cur_desc, list(deps), self.cur_kind))
        self._emit_deps(engine, deps)
        inst = fn()
        try:
            iname = type(inst.ins).__name__
        except Exception:
            iname = "?"
        g = self.graph[-1]
        self.graph[-1] = (g[0], g[1], g[2], g[3], iname)
        self.counts[engine] += 1
        n = self.counts[engine]
        inst.then_inc(self.sem[engine], 1)
        ev = ("s:" + engine, n)
        self.ev_desc[ev] = self.cur_desc
        self.graph[-1] = (ev,) + self.graph[-1][1:]
        self._finish_op(engine, ev, rw, clock_extra={"s:" + engine: n})
        return ev

    def dma(self, chname, fn, reads=(), writes=()):
        ch = self.chan[chname]
        engine = ch["engine"]
        deps, rw = self._collect(reads, writes)
        if ch["count"] > 0:
            deps.append(("c:" + chname, 16 * ch["count"]))
        self.graph.append((None, "dma:" + chname, self.cur_desc, list(deps),
                           self.cur_kind))
        self._emit_deps(engine, deps)
        inst = fn()
        ch["count"] += 1
        inst.then_inc(ch["sem"], 16)
        ev = ("c:" + chname, 16 * ch["count"])
        self.graph[-1] = (ev,) + self.graph[-1][1:]
        self._finish_op(engine, ev, rw)
        return ev

    def matmul_group(self, psum_ap, pairs):
        n = len(pairs)
        ev = None
        for i, (lh, rh) in enumerate(pairs):
            ev = self.op(
                "tensor",
                lambda lh=lh, rh=rh, i=i: self.nc.tensor.matmul(
                    psum_ap, lh, rh, start=(i == 0), stop=(i == n - 1)
                ),
                reads=[lh, rh],
                writes=[psum_ap] if i in (0, n - 1) else (),
            )
        return ev

    def finish(self, engine="sync"):
        for o in ENGINES:
            if self.counts[o] > 0 and o != engine:
                self._wait(engine, ("s:" + o, self.counts[o]))
        for nm, ch in self.chan.items():
            if ch["count"] > 0:
                self._wait(engine, ("c:" + nm, 16 * ch["count"]))


def mkap(handle, off, dims):
    base = handle[:]
    return bass.AP(base.tensor, off, dims)


W_ORDER = ["wv", "wi", "wf", "wo", "wr", "wk", "wv2"]


def build_program(serialize=False):
    nc = bass.Bass("TRN2", target_bir_lowering=False,
                   dynamic_dma_scratch_size=512)
    xt = nc.dram_tensor("xt", [1024, NSEQ * 2048], F32, kind="ExternalInput").ap()
    wdr = {w: nc.dram_tensor(w, [1024, 1024], F32, kind="ExternalInput").ap()
           for w in W_ORDER}
    mur_d = nc.dram_tensor("mur", [P, NCH], F32, kind="ExternalInput").ap()
    muk_d = nc.dram_tensor("muk", [P, NCH], F32, kind="ExternalInput").ap()
    yt = nc.dram_tensor("yt", [1024, NSEQ * 2048], F32, kind="ExternalOutput").ap()

    p = Prog(nc, serialize=serialize)
    es = ExitStack()
    with es:
        p.setup(es)
        sb_ = lambda n, s, d=F32: es.enter_context(nc.sbuf_tensor(n, s, d))

        ws = {w: sb_("ws_" + w, [P, NCH * 1024], BF16) for w in W_ORDER}
        S0s = [sb_(f"s0_{i}", [P, NCH * PITCH]) for i in range(3)]
        S1s = [sb_(f"s1_{i}", [P, NCH * PITCH]) for i in range(2)]
        S2s = [sb_(f"s2_{i}", [P, NCH * PITCH]) for i in range(2)]
        S3s = [sb_(f"s3_{i}", [P, NCH * PITCH]) for i in range(2)]
        XBs = [sb_(f"xb_{i}", [P, FD], BF16) for i in range(2)]
        B1s = [sb_(f"b1_{i}", [P, FD], BF16) for i in range(2)]
        B2s = [sb_(f"b2_{i}", [P, FD], BF16) for i in range(2)]
        Xbs = [sb_(f"xrow_{i}", [P, TC]) for i in range(2)]
        Tbs = [sb_(f"trow_{i}", [P, TC]) for i in range(2)]
        Ybs = [sb_(f"yrow_{i}", [P, TC]) for i in range(2)]
        Qibs = [sb_(f"qib_{i}", [P, TC], I32) for i in range(2)]
        qC = sb_("qc", [P, TC], I32)
        ones128 = sb_("ones128", [P, P], BF16)
        eps_row = sb_("eps_row", [1, TC])
        mur = sb_("mur_s", [P, NCH])
        muk = sb_("muk_s", [P, NCH])
        stc = sb_("stc", [P, NSEQ * NCH])
        ybc = sb_("ybc", [P, NSEQ * NCH])
        onesb = sb_("onesb", [P, 1], BF16)
        onesf = sb_("onesf", [1, P])

        def ps_(n, s):
            t = es.enter_context(nc.psum_tensor(n, s, F32))
            p.psum_full[n] = 2048 * 4
            return t

        pmrot = [ps_(f"pm{i}", [P, 512]) for i in range(6)]
        pmA = ps_("pmA", [P, 512])
        pmB = ps_("pmB", [P, 512])

        rot_i = [0]

        def next_rot():
            t = pmrot[rot_i[0] % 6]
            rot_i[0] += 1
            return t[:, 0:TC]

        v, a, g, sy = nc.vector, nc.scalar, nc.gpsimd, nc.sync

        def flat(slot, n=None):
            if n is None:
                return slot[:, 0:FD]
            return slot[:, n * TC:(n + 1) * TC]

        def flat3(slot):
            return flat(slot).rearrange("p (a t) -> p a t", a=NCH)

        def flat_k(ap2d, k=NCH):
            return ap2d.rearrange("p (k t) -> p k t", k=k)

        p.op("vector", lambda: v.memset(onesb[:], 1.0), [], [onesb[:]])
        p.op("vector", lambda: v.memset(onesf[:], 1.0), [], [onesf[:]])
        p.op("vector", lambda: v.memset(ones128[:], 2.0 ** -10), [], [ones128[:]])
        p.op("vector", lambda: v.memset(qC[:], QUAKE), [], [qC[:]])
        p.op("vector", lambda: v.memset(eps_row[:], HEPS), [], [eps_row[:]])
        cmu = p.channel("cmu", "sync")
        p.dma(cmu, lambda: sy.dma_start(out=mur[:], in_=mur_d), [mur_d], [mur[:]])
        p.dma(cmu, lambda: sy.dma_start(out=muk[:], in_=muk_d), [muk_d], [muk[:]])

        xin = [p.channel(f"xin{i}", "sync") for i in range(3)]
        stq = [p.channel(f"st{i}", "sync") for i in range(3)]
        wl = [p.channel("wl0", "sync"), p.channel("wl1", "sync")]

        def x_src(base):
            return xt[:, base:base + TC].rearrange("(k p) t -> p k t", p=P)

        def y_dst(base):
            return yt[:, base:base + TC].rearrange("(k p) t -> p k t", p=P)

        p.dma(xin[0], lambda: sy.dma_start(out=flat_k(flat(S0s[0])),
                                           in_=x_src(0)),
              [xt], [flat(S0s[0])])
        p.op("vector", lambda: v.tensor_copy(XBs[0][:], flat(S0s[0])),
             [flat(S0s[0])], [XBs[0][:]])
        p.dma(xin[1], lambda: sy.dma_start(out=flat_k(flat(S0s[1])),
                                           in_=x_src(2048)),
              [xt], [flat(S0s[1])])

        for i, w in enumerate(W_ORDER):
            for q in range(4):
                j = 4 * i + q
                stage = [S2s[0], S3s[0], S2s[1], S3s[1]][j % 4]
                src = wdr[w][256 * q:256 * (q + 1), :].rearrange(
                    "(k p) n -> p k n", p=P)
                ch = wl[j % 2]
                qeng = sy
                p.dma(ch, lambda qeng=qeng, stage=stage, src=src:
                      qeng.dma_start(out=flat_k(flat(stage), k=2), in_=src),
                      [wdr[w]], [flat(stage)])
                dst = ws[w][:, q * 2048:(q + 1) * 2048]
                if w == "wv2":
                    p.op("vector", lambda dst=dst, stage=stage:
                         v.tensor_scalar_mul(dst, flat(stage), 0.5),
                         [flat(stage)], [dst])
                else:
                    p.op("vector", lambda dst=dst, stage=stage:
                         v.tensor_copy(dst, flat(stage)), [flat(stage)], [dst])

        def mm_into(wname, m, rhs_buf, psum_ap):
            pairs = []
            for k in range(NCH):
                lh = ws[wname][:, k * 1024 + m * P: k * 1024 + (m + 1) * P]
                rh = rhs_buf[:, k * TC:(k + 1) * TC]
                pairs.append((lh, rh))
            return p.matmul_group(psum_ap, pairs)

        def bufs(c):
            pa = c % 2
            return dict(
                S0=S0s[c % 3], S1=S1s[pa], S2=S2s[pa], S3=S3s[pa],
                XB=XBs[pa], B1=B1s[pa], B2=B2s[pa],
                Xb=Xbs[pa], Tb=Tbs[pa], Yb=Ybs[pa], Qib=Qibs[pa],
            )

        def chunk_pos(c):
            w_i, b = divmod(c, NSEQ)
            return b, w_i

        def y1aps(S0):
            cur = mkap(S0, 1, [[NCH * PITCH, P], [PITCH, NCH], [1, TC]])
            prev = mkap(S0, 0, [[NCH * PITCH, P], [PITCH, NCH], [1, TC]])
            bnd = mkap(S0, 0, [[NCH * PITCH, P], [PITCH, NCH], [1, 1]])
            last = mkap(S0, TC, [[NCH * PITCH, P], [PITCH, NCH], [1, 1]])
            full = S0[:, 0:NCH * PITCH]
            return cur, prev, bnd, last, full

        def stage_A1(c, __name='stage_A1'):
            p.cur_desc = f'stage_A1({c})'
            b, w_i = chunk_pos(c)
            base = b * 2048 + w_i * TC
            S0 = bufs(c)["S0"]
            p.dma(xin[c % 3], lambda S0=S0, base=base: sy.dma_start(
                out=flat_k(flat(S0)), in_=x_src(base)), [xt], [flat(S0)])

        def stage_A2(c, __name='stage_A2'):
            p.cur_desc = f'stage_A2({c})'
            B = bufs(c)
            S0, XB = B["S0"], B["XB"]
            for qq in range(4):
                lo, hi = qq * 2 * TC, (qq + 1) * 2 * TC
                p.op("vector", lambda S0=S0, XB=XB, lo=lo, hi=hi: v.tensor_copy(
                    XB[:, lo:hi], flat(S0)[:, lo:hi]),
                     [flat(S0)[:, lo:hi]], [XB[:, lo:hi]])

        def stage_VIF(c, __name='stage_VIF'):
            p.cur_desc = f'stage_VIF({c})'
            B = bufs(c)
            S0, S1, S2, S3, XB = B["S0"], B["S1"], B["S2"], B["S3"], B["XB"]
            TW = S3s[(c + 1) % 2]
            b, w_i = chunk_pos(c)
            for n in range(NCH):
                pmV = next_rot()
                mm_into("wv", n, XB[:], pmV)
                p.op("scalar", lambda n=n, pmV=pmV: a.activation(
                    flat(S1, n), pmV, AF.Silu), [pmV], [flat(S1, n)])
                pmI = next_rot()
                mm_into("wi", n, XB[:], pmI)
                p.op("scalar", lambda n=n, pmI=pmI: a.activation(
                    flat(TW, n), pmI, AF.Tanh, scale=0.5),
                    [pmI], [flat(TW, n)])
                p.op("vector", lambda n=n: v.scalar_tensor_tensor(
                    flat(S2, n), flat(TW, n), 1.0, flat(S1, n),
                    ALU.add, ALU.mult),
                    [flat(TW, n), flat(S1, n)], [flat(S2, n)])
                pmF = next_rot()
                mm_into("wf", n, XB[:], pmF)
                p.op("scalar", lambda n=n, pmF=pmF: a.activation(
                    flat(S3, n), pmF, AF.Tanh), [pmF], [flat(S3, n)])
            for n in range(NCH):
                init = 0.0 if w_i == 0 else stc[:, b * NCH + n: b * NCH + n + 1]
                rd = [flat(S3, n), flat(S2, n)]
                if w_i != 0:
                    rd.append(stc[:, b * NCH + n: b * NCH + n + 1])
                p.op("vector", lambda n=n, init=init: v.tensor_tensor_scan(
                    flat(S3, n), flat(S3, n), flat(S2, n), init,
                    ALU.mult, ALU.add), rd, [flat(S3, n)])
                p.op("scalar", lambda n=n: a.activation(
                    B["B1"][:, n * TC:(n + 1) * TC], flat(S3, n), AF.Silu,
                    scale=0.5),
                     [flat(S3, n)], [B["B1"][:, n * TC:(n + 1) * TC]])
            if w_i != NW - 1:
                sco = mkap(S3, TC - 1, [[NCH * PITCH, P], [TC, NCH], [1, 1]])
                std = mkap(stc, b * NCH, [[NSEQ * NCH, P], [1, NCH], [1, 1]])
                p.op("vector", lambda sco=sco, std=std: v.tensor_copy(std, sco),
                     [flat(S3)], [stc[:, b * NCH:(b + 1) * NCH]])

        def stage_O(c, __name='stage_O'):
            p.cur_desc = f'stage_O({c})'
            B = bufs(c)
            S0, S2 = B["S0"], B["S2"]
            for h in range(NCH):
                pm = next_rot()
                mm_into("wo", h, B["B1"][:], pm)
                p.op("vector", lambda h=h, pm=pm: v.tensor_tensor(
                    flat(S2, h), pm, flat(S0, h), ALU.add),
                    [pm, flat(S0, h)], [flat(S2, h)])

        def ln_stats(c, src_slot, hbf, sqb, B, bank):
            Xr, Tr, Yr, Qi = B["Xb"][:], B["Tb"][:], B["Yb"][:], B["Qib"][:]
            m_bc = bank[:, 0:TC]
            q_bc = bank[:, TC:2 * TC]
            hh = NCH // 2
            p.op("scalar", lambda hh=hh: a.copy(
                hbf[:, 0:hh * TC], flat(src_slot)[:, 0:hh * TC]),
                 [flat(src_slot)[:, 0:hh * TC]], [hbf[:, 0:hh * TC]])
            p.op("scalar", lambda hh=hh: a.copy(
                hbf[:, hh * TC:FD], flat(src_slot)[:, hh * TC:FD]),
                 [flat(src_slot)[:, hh * TC:FD]], [hbf[:, hh * TC:FD]])
            p.op("scalar", lambda hh=hh: a.activation(
                sqb[:, 0:hh * TC], flat(src_slot)[:, 0:hh * TC], AF.Square),
                 [flat(src_slot)[:, 0:hh * TC]], [sqb[:, 0:hh * TC]])
            p.op("scalar", lambda hh=hh: a.activation(
                sqb[:, hh * TC:FD], flat(src_slot)[:, hh * TC:FD], AF.Square),
                 [flat(src_slot)[:, hh * TC:FD]], [sqb[:, hh * TC:FD]])
            p.matmul_group(m_bc, [(ones128[:], hbf[:, k * TC:(k + 1) * TC])
                                  for k in range(NCH)])
            p.matmul_group(q_bc, [(ones128[:], sqb[:, k * TC:(k + 1) * TC])
                                  for k in range(NCH)]
                           + [(onesf[:], eps_row[:])])
            p.op("scalar", lambda: a.activation(Tr, m_bc, AF.Square),
                 [m_bc], [Tr])
            p.op("vector", lambda: v.scalar_tensor_tensor(
                Xr, Tr, -1.0, q_bc, ALU.mult, ALU.add), [Tr, q_bc], [Xr])
            p.op("vector", lambda: v.tensor_single_scalar(
                Qi, Xr.bitcast(I32), 1, ALU.arith_shift_right), [Xr], [Qi])
            p.op("vector", lambda: v.tensor_tensor(Qi, qC[:], Qi, ALU.subtract),
                 [qC[:], Qi], [Qi])
            ysrc = Qi.bitcast(F32)
            for it in range(3):
                yin = ysrc if it == 0 else Yr
                yrd = [Qi] if it == 0 else [Yr]
                p.op("vector", lambda yin=yin, yrd=yrd: v.tensor_tensor(
                    Tr, yin, yin, ALU.mult), yrd, [Tr])
                p.op("vector", lambda: v.scalar_tensor_tensor(
                    Tr, Tr, -0.5, Xr, ALU.mult, ALU.mult), [Tr, Xr], [Tr])
                p.op("vector", lambda yin=yin, yrd=yrd: v.scalar_tensor_tensor(
                    Yr, Tr, 1.5, yin, ALU.add, ALU.mult),
                    [Tr] + yrd, [Yr])

        def ln_apply(c, src_slot, B, out3, out_writes, bank):
            Yr = B["Yb"][:]
            src3 = flat3(src_slot)
            bm3 = bank[:, 0:TC].rearrange("p (a t) -> p a t", a=1)
            yr3 = Yr.rearrange("p (a t) -> p a t", a=1)
            s3b, bm3b = bass.broadcast_tensor_aps(src3, bm3)
            p.op("vector", lambda: v.tensor_tensor(s3b, s3b, bm3b,
                                                   ALU.subtract),
                 [flat(src_slot), bank[:, 0:TC]], [flat(src_slot)])
            t3b, yr3b = bass.broadcast_tensor_aps(src3, yr3)
            p.op("vector", lambda: v.tensor_tensor(out3, t3b, yr3b, ALU.mult),
                 [flat(src_slot), Yr], out_writes)

        def stage_E(c, __name='stage_E'):
            p.cur_desc = f'stage_E({c})'
            B = bufs(c)
            ln_stats(c, B["S2"], B["B2"], B["B1"], B, pmA)

        def stage_F(c, __name='stage_F'):
            p.cur_desc = f'stage_F({c})'
            B = bufs(c)
            S0, S3, XB, B2 = B["S0"], B["S3"], B["XB"], B["B2"]
            b, w_i = chunk_pos(c)
            y1cur, y1prev, y1bnd, y1last, y1full = y1aps(S0)
            ln_apply(c, B["S2"], B, y1cur, [y1full], pmA)
            if w_i == 0:
                p.op("gpsimd", lambda: g.memset(y1bnd, 0.0), [], [y1full])
            else:
                ybs = mkap(ybc, b * NCH, [[NSEQ * NCH, P], [1, NCH], [1, 1]])
                p.op("gpsimd", lambda ybs=ybs: g.tensor_copy(y1bnd, ybs),
                     [ybc[:, b * NCH:(b + 1) * NCH]], [y1full])
            S1 = B["S1"]
            half = NCH // 2
            y1c_a = mkap(S0, 1, [[NCH * PITCH, P], [PITCH, half], [1, TC]])
            y1p_a = mkap(S0, 0, [[NCH * PITCH, P], [PITCH, half], [1, TC]])
            y1c_b = mkap(S0, half * PITCH + 1,
                         [[NCH * PITCH, P], [PITCH, half], [1, TC]])
            y1p_b = mkap(S0, half * PITCH,
                         [[NCH * PITCH, P], [PITCH, half], [1, TC]])
            d3a = flat(S1)[:, 0:half * TC].rearrange("p (a t) -> p a t", a=half)
            d3b = flat(S1)[:, half * TC:FD].rearrange("p (a t) -> p a t", a=half)
            p.op("vector", lambda: v.tensor_tensor(
                d3a, y1c_a, y1p_a, ALU.subtract),
                [y1full], [flat(S1)[:, 0:half * TC]])
            p.op("vector", lambda: v.tensor_tensor(
                d3b, y1c_b, y1p_b, ALU.subtract),
                [y1full], [flat(S1)[:, half * TC:FD]])
            for n in range(NCH):
                yp = mkap(S0, n * PITCH, [[NCH * PITCH, P], [1, TC]])
                p.op("vector", lambda n=n, yp=yp: v.scalar_tensor_tensor(
                    B2[:, n * TC:(n + 1) * TC], flat(S1, n),
                    mur[:, n:n + 1], yp, ALU.mult, ALU.add),
                    [flat(S1, n), mur[:, n:n + 1], y1full],
                    [B2[:, n * TC:(n + 1) * TC]])
            for n in range(NCH):
                yp = mkap(S0, n * PITCH, [[NCH * PITCH, P], [1, TC]])
                p.op("vector", lambda n=n, yp=yp: v.scalar_tensor_tensor(
                    XB[:, n * TC:(n + 1) * TC], flat(S1, n),
                    muk[:, n:n + 1], yp, ALU.mult, ALU.add),
                    [flat(S1, n), muk[:, n:n + 1], y1full],
                    [XB[:, n * TC:(n + 1) * TC]])
            if w_i != NW - 1:
                ybd = mkap(ybc, b * NCH, [[NSEQ * NCH, P], [1, NCH], [1, 1]])
                p.op("gpsimd", lambda ybd=ybd: g.tensor_copy(ybd, y1last),
                     [y1full], [ybc[:, b * NCH:(b + 1) * NCH]])

        def stage_GH_R(c, __name='stage_GH_R'):
            p.cur_desc = f'stage_GH_R({c})'
            B = bufs(c)
            S0, S1, S2, S3, XB = B["S0"], B["S1"], B["S2"], B["S3"], B["XB"]
            for n in range(NCH):
                pm = next_rot()
                mm_into("wr", n, B["B2"][:], pm)
                p.op("scalar", lambda n=n, pm=pm: a.activation(
                    flat(S1, n), pm, AF.Tanh, scale=0.5), [pm], [flat(S1, n)])
            for n in range(NCH):
                pm = next_rot()
                mm_into("wk", n, XB[:], pm)
                p.op("scalar", lambda n=n, pm=pm: a.activation(
                    flat(S2, n), pm, AF.Relu), [pm], [flat(S2, n)])
                p.op("scalar", lambda n=n: a.activation(
                    B["B1"][:, n * TC:(n + 1) * TC], flat(S2, n), AF.Square),
                    [flat(S2, n)], [B["B1"][:, n * TC:(n + 1) * TC]])
        def stage_GH_V(c, __name='stage_GH_V'):
            p.cur_desc = f'stage_GH_V({c})'
            B = bufs(c)
            S0, S1, S2, S3 = B["S0"], B["S1"], B["S2"], B["S3"]
            for d in range(NCH):
                pm = next_rot()
                mm_into("wv2", d, B["B1"][:], pm)
                p.op("vector", lambda d=d, pm=pm: v.scalar_tensor_tensor(
                    flat(S3, d), flat(S1, d), 1.0, pm, ALU.add, ALU.mult),
                    [pm, flat(S1, d)], [flat(S3, d)])
                yc = mkap(S0, d * PITCH + 1, [[NCH * PITCH, P], [1, TC]])
                p.op("gpsimd", lambda d=d, yc=yc: g.tensor_tensor(
                    flat(S2, d), flat(S3, d), yc, ALU.add),
                    [flat(S3, d), S0[:, 0:NCH * PITCH]], [flat(S2, d)])

        def stage_I(c, __name='stage_I'):
            p.cur_desc = f'stage_I({c})'
            B = bufs(c)
            ln_stats(c, B["S2"], B["B2"], B["B1"], B, pmB)

        def stage_JK(c, __name='stage_JK'):
            p.cur_desc = f'stage_JK({c})'
            B = bufs(c)
            S0 = B["S0"]
            b, w_i = chunk_pos(c)
            base = b * 2048 + w_i * TC
            ln_apply(c, B["S2"], B, flat3(S0), [flat(S0)], pmB)
            p.dma(stq[c % 3],
                  lambda base=base, S0=S0: sy.dma_start(out=y_dst(base),
                                                        in_=flat_k(flat(S0))),
                  [flat(S0)], [yt])

        for c in range(NCHUNK + 1):
            if c < NCHUNK:
                stage_VIF(c)
            if 1 <= c <= NCHUNK:
                stage_GH_R(c - 1)
            if c + 1 < NCHUNK:
                stage_A2(c + 1)
            if c < NCHUNK:
                stage_O(c)
            if 1 <= c <= NCHUNK:
                stage_GH_V(c - 1)
            if c < NCHUNK:
                stage_E(c)
            if c < NCHUNK:
                stage_F(c)
            if 1 <= c <= NCHUNK:
                stage_I(c - 1)
            if 1 <= c <= NCHUNK:
                stage_JK(c - 1)
            if c + 2 < NCHUNK:
                stage_A1(c + 2)

        p.finish("sync")
    return nc, p


def kernel(L0_attn_x, L0_attn_Wf, L0_attn_Wi, L0_attn_Wv, L0_attn_Wo,
           L0_ffn_W_r, L0_ffn_W_k, L0_ffn_W_v, L0_ffn_mu_r, L0_ffn_mu_k):
    x = np.asarray(L0_attn_x, dtype=np.float32)
    wT = {
        "wf": np.ascontiguousarray(np.asarray(L0_attn_Wf, np.float32).T),
        "wi": np.ascontiguousarray(np.asarray(L0_attn_Wi, np.float32).T),
        "wv": np.ascontiguousarray(np.asarray(L0_attn_Wv, np.float32).T),
        "wo": np.ascontiguousarray(np.asarray(L0_attn_Wo, np.float32).T),
        "wr": np.ascontiguousarray(np.asarray(L0_ffn_W_r, np.float32).T),
        "wk": np.ascontiguousarray(np.asarray(L0_ffn_W_k, np.float32).T),
        "wv2": np.ascontiguousarray(np.asarray(L0_ffn_W_v, np.float32).T),
    }
    mur = np.ascontiguousarray(
        np.asarray(L0_ffn_mu_r, np.float32).reshape(NCH, P).T)
    muk = np.ascontiguousarray(
        np.asarray(L0_ffn_mu_k, np.float32).reshape(NCH, P).T)

    nc, _ = build_program()
    in_maps = []
    for c in range(8):
        xtc = np.ascontiguousarray(
            x[4 * c:4 * c + 4].transpose(2, 0, 1).reshape(1024, NSEQ * 2048))
        m = dict(xt=xtc, mur=mur, muk=muk)
        m.update(wT)
        in_maps.append(m)
    res = run_bass_kernel_spmd(nc, in_maps, core_ids=list(range(8)))
    outs = []
    for c in range(8):
        ytc = np.asarray(res.results[c]["yt"])
        outs.append(ytc.reshape(1024, NSEQ, 2048).transpose(1, 2, 0))
    return np.ascontiguousarray(np.concatenate(outs, axis=0))
